# version 14
# speedup vs baseline: 1.1838x; 1.0051x over previous
import contextlib
from collections import defaultdict

import numpy as np
import ml_dtypes

import concourse.bass as bass
import concourse.mybir as mybir
from concourse.bass_utils import run_bass_kernel_spmd

F32 = mybir.dt.float32
BF16 = mybir.dt.bfloat16
AF = mybir.ActivationFunctionType
ALU = mybir.AluOpType
AX = mybir.AxisListType

D = 1024
NCH = 8
HID = 4096
NEX = 20
EPS_LN = 1e-5
EPS_ATTN = 1e-6
EPS_RTR = 1e-9
C_EPSLN, C_EPSATTN, C_EPSRTR, C_NEG1, C_NEGH, C_RN, C_ZERO, C_POSH = range(8)


class Prog:
    def __init__(self):
        self.ops = {e: [] for e in ("sync", "tensor", "vector", "scalar", "gpsimd")}
        self.exp = defaultdict(int)
        self.sem_names = set()
        self.sidx = {}

    ENG_SEM = {"vector": "dve", "scalar": "act", "gpsimd": "pool"}

    def emit(self, engine, builder, waits=(), inc=None, amt=1, chain=True):
        waits = [w for w in waits if w is not None]
        es = self.ENG_SEM.get(engine)
        if es is not None and chain:
            if self.exp[es]:
                waits.append((es, self.exp[es]))
            if inc is None:
                inc = es
                amt = 1
        tok = None
        if inc is not None:
            self.sem_names.add(inc)
            self.exp[inc] += amt
            tok = (inc, self.exp[inc])
        for s, _v in waits:
            self.sem_names.add(s)
        self.ops[engine].append((builder, list(waits), inc, amt))
        return tok

    def dma(self, engine, builder, waits=(), inc="d", nsem=2):
        idx = self.sidx[inc] = self.sidx.get(inc, -1) + 1
        sem = f"{inc}@{idx % nsem}"
        prev = (sem, self.exp[sem]) if self.exp[sem] else None
        return self.emit(engine, builder, waits=list(waits) + [prev],
                         inc=sem, amt=16, chain=False)

    def stream_last(self, stream):
        return [self.last(f"{stream}@{j}") for j in range(4)
                if self.exp.get(f"{stream}@{j}")]

    def last(self, sem):
        return (sem, self.exp[sem]) if self.exp[sem] else None

    def make_sems(self, nc, stack):
        return {n: stack.enter_context(nc.semaphore(n)) for n in sorted(self.sem_names)}

    def build(self, block, sems):
        last_wait = {e: defaultdict(int) for e in self.ops}

        def body(eng, ename):
            for builder, waits, inc, amt in self.ops[ename]:
                for s, v in waits:
                    if last_wait[ename][s] < v:
                        eng.wait_ge(sems[s], v)
                        last_wait[ename][s] = v
                inst = builder(eng)
                if inc is not None:
                    assert inst is not None
                    inst.then_inc(sems[inc], amt)

        for ename, deco in (("sync", block.sync), ("tensor", block.tensor),
                            ("vector", block.vector), ("scalar", block.scalar),
                            ("gpsimd", block.gpsimd)):
            if self.ops[ename]:
                deco(lambda eng, _e=ename: body(eng, _e))


class Mem:
    def __init__(self, nc, base, top):
        self.nc, self.base, self.top, self.cur = nc, base, top, base

    def alloc(self, name, shape, dtype, align=64):
        nbytes = int(np.prod(shape[1:])) * (2 if dtype == BF16 else 4)
        off = (self.cur + align - 1) // align * align
        assert off + nbytes <= self.top, (
            f"SBUF overflow: {name} needs {nbytes}B at {off}, top {self.top}")
        self.cur = off + nbytes
        return self.nc.alloc_sbuf_tensor_at(name, list(shape), dtype, offset=off)


class PS:

    def __init__(self, p, pb):
        self.p, self.pb = p, pb
        self.rd = {}
        self.started = [False] * 8

    def wr_waits(self, banks):
        if isinstance(banks, int):
            banks = [banks]
        out = []
        for b in banks:
            out += self.rd.get(b, [])
        return out

    def note_read(self, bank, tok, add=False):
        if add and bank in self.rd:
            self.rd[bank].append(tok)
        else:
            self.rd[bank] = [tok]


def _t3(ap, inner):
    return ap.rearrange("p (a t) -> p a t", t=inner)


def build_core(seq=8192, taps=(), stop_after=None, b2_tiles=None):
    nc = bass.Bass("TRN2", target_bir_lowering=False, debug=False)
    T = seq // 2
    NT_A = seq // 512
    NT_O = T // 512
    NT_C = T // 1024
    NT_B2 = T // 256

    def din(name, shape, dtype=F32):
        return nc.dram_tensor(name, list(shape), dtype, kind="ExternalInput").ap()

    xT = din("xT", [D, seq])
    wqT = din("wqT", [D, D]); wkvT = din("wkvT", [D, 2 * D]); projT = din("projT", [D, D])
    fc1T = din("fc1T", [D, HID]); fc2T = din("fc2T", [HID, D])
    rw = din("rw", [D, 16])
    wg = din("wg", [NEX, D, 256]); wu = din("wu", [NEX, D, 256]); wd = din("wd", [NEX, 256, D])
    b_n1g = din("b_n1g", [128, 8]); b_n1b = din("b_n1b", [128, 8])
    b_n2g = din("b_n2g", [128, 8]); b_n2b = din("b_n2b", [128, 8])
    b_pj = din("b_pj", [128, 8]); b_fc2 = din("b_fc2", [128, 8])
    b_fc1 = din("b_fc1", [128, 32])
    c_f32d = din("c_f32", [128, 8])
    c_identd = din("c_ident", [128, 128])
    c_ones1d = din("c_ones1", [128, 1], BF16)
    c_ones33d = din("c_ones33", [33, 128], BF16)
    c_indzd = din("c_indz", [48, NCH * 128], BF16)
    c_inded = din("c_inde", [16, 16 * 128], BF16)
    c_pmd = din("c_pm", [33, 2], BF16)
    c_zrd = din("c_zr", [1, 128], BF16)
    c_i1d = din("c_i1", [1, 1], mybir.dt.int32)
    out_d = nc.dram_tensor("out", [D, T], F32, kind="ExternalOutput").ap()

    def scr(name, shape, dtype):
        kind = "ExternalOutput" if name in taps else "Internal"
        return nc.dram_tensor(name, list(shape), dtype, kind=kind).ap()

    y1sp = scr("y1sp", [128, NCH * T], BF16)
    gsp = scr("gsp", [128, NCH * T], BF16)
    tap_out = {}
    for tname, shape, dt in [("t_xn", [128, NCH * T], BF16),
                             ("t_kvb", [128, NCH * 128], BF16),
                             ("t_ksm", [128, NCH * 16], BF16),
                             ("t_comb", [16, T], BF16),
                             ("t_h1", [128, NCH * 512], BF16),
                             ("t_fq", [128, NCH * 512], BF16)]:
        if tname in taps:
            tap_out[tname] = nc.dram_tensor(tname, shape, dt, kind="ExternalOutput").ap()

    p = Prog()
    base0 = (nc.sbuf_base + 63) // 64 * 64
    mem = Mem(nc, base0, nc.sbuf_top)

    c_f32 = mem.alloc("c_f32s", [128, 8], F32)
    ident = mem.alloc("ident", [128, 128], F32)
    ones1 = mem.alloc("ones1", [128, 1], BF16)
    ones33 = mem.alloc("ones33", [33, 128], BF16)
    indz = mem.alloc("indz", [48, NCH * 128], BF16)
    inde = mem.alloc("inde", [16, 16 * 128], BF16)
    c_pm = mem.alloc("c_pm", [33, 2], BF16)
    c_zr = mem.alloc("c_zr", [1, 128], BF16)
    c_i1 = mem.alloc("c_i1", [1, 1], mybir.dt.int32)
    g1c = mem.alloc("g1c", [128, 8], F32); b1c = mem.alloc("b1c", [128, 8], F32)
    g2c = mem.alloc("g2c", [128, 8], F32); b2c = mem.alloc("b2c", [128, 8], F32)
    pjc = mem.alloc("pjc", [128, 8], F32); f2c = mem.alloc("f2c", [128, 8], F32)
    f1c = mem.alloc("f1c", [128, 32], F32)
    fixed_end = mem.cur

    zx0 = mem.cur
    kvblk = mem.alloc("kvblk", [128, NCH * 128], BF16)
    ksmat = mem.alloc("ksmat", [128, NCH * 16], BF16)
    rw_b = mem.alloc("rw_b", [128, NCH * 16], BF16)
    r1store = mem.alloc("r1store", [33, 8 * 512], BF16)
    zx = mem.cur
    xn_b = mem.alloc("xn_b", [128, NCH * T], BF16)
    arena0 = (mem.cur + 63) // 64 * 64
    czone = Mem(nc, arena0, nc.sbuf_top)
    comb_fm = czone.alloc("comb_fm", [16, T], BF16)
    az = (czone.cur + 63) // 64 * 64
    b2_base = (zx0 + 63) // 64 * 64

    stack = contextlib.ExitStack()
    pb = [stack.enter_context(nc.psum_tensor(f"pb{i}", [128, 512], F32)) for i in range(8)]
    ps = PS(p, pb)

    def mm(t, bank, out_ap, lhsT, rhs, first, last, fresh=True):
        st = first and (fresh or not ps.started[bank])
        if first:
            ps.started[bank] = True
        return t.matmul(out_ap, lhsT, rhs, start=st, stop=last)

    def cs(col, rows=128):
        return c_f32.ap()[0:rows, col:col + 1]

    fpm = mem.alloc("fpm", [1, 12], F32)
    fA = fpm.ap()[0:1, 0:4]
    fB = fpm.ap()[0:1, 4:8]
    fC = fpm.ap()[0:1, 8:12]
    tf1 = p.emit("vector", lambda v: v.memset(fA, 1073741824.0))
    tf2 = p.emit("vector", lambda v: v.memset(fB, 10.0))
    tf3 = p.emit("vector", lambda v: v.tensor_tensor(fC, fA, fB, op=ALU.add),
                 waits=[tf1, tf2])
    tf4 = p.emit("vector", lambda v: v.tensor_tensor(fC, fC, fA, op=ALU.subtract),
                 waits=[tf3])
    p.emit("scalar", lambda s: s.activation(fC, fC, AF.Sin), waits=[tf4])

    for dst, src in [(c_f32, c_f32d), (ident, c_identd), (ones1, c_ones1d),
                     (ones33, c_ones33d), (indz, c_indzd), (inde, c_inded),
                     (c_pm, c_pmd), (c_zr, c_zrd), (c_i1, c_i1d), (g1c, b_n1g), (b1c, b_n1b), (g2c, b_n2g), (b2c, b_n2b),
                     (pjc, b_pj), (f2c, b_fc2), (f1c, b_fc1)]:
        p.dma("sync", (lambda d, s: lambda e: e.dma_start(out=d.ap(), in_=s))(dst, src),
              inc="cld", nsem=1)
    tok_cld = p.last("cld@0")

    def ln_rows(bank_sum, bank_sq, m2row, r2row, rowf, stat_toks, extra_waits, width=512,
                preserve_var=False):
        mf = rowf.ap()[0:1, 0:width]
        vf = rowf.ap()[0:1, width:2 * width]
        rf = rowf.ap()[0:1, 2 * width:3 * width] if preserve_var else vf
        tA = p.emit("vector", lambda v: v.tensor_scalar_mul(mf, pb[bank_sum][0:1, 0:width], 1.0 / D),
                    waits=[stat_toks[0]] + list(extra_waits), inc="dve")
        p.emit("vector", lambda v: v.tensor_tensor(vf, mf, mf, op=ALU.mult), inc="dve")
        tC = p.emit("vector", lambda v: v.scalar_tensor_tensor(
            vf, pb[bank_sq][0:1, 0:width], 1.0 / D, vf, op0=ALU.mult, op1=ALU.subtract),
            waits=[stat_toks[1]], inc="dve")
        p.emit("scalar", lambda s: s.activation(rf, vf, AF.Ln, bias=cs(C_EPSLN, 1)),
               waits=[tC], inc="act")
        tE = p.emit("scalar", lambda s: s.activation(rf, rf, AF.Exp, scale=cs(C_NEGH, 1)),
                    inc="act")
        p.emit("vector", lambda v: v.tensor_copy(m2row.ap()[0:1, 0:width], mf),
               waits=[tA], inc="dve")
        t2 = p.emit("vector", lambda v: v.tensor_tensor(
            m2row.ap()[32:33, 0:width], mf, m2row.ap()[0:1, 0:width], op=ALU.subtract),
            inc="dve")
        p.emit("vector", lambda v: v.tensor_copy(r2row.ap()[0:1, 0:width], rf),
               waits=[tE], inc="dve")
        t4 = p.emit("vector", lambda v: v.tensor_tensor(
            r2row.ap()[32:33, 0:width], rf, r2row.ap()[0:1, 0:width], op=ALU.subtract),
            inc="dve")
        ps.note_read(bank_sum, t2)
        ps.note_read(bank_sq, t4)
        return t2, t4, tE

    def ln_bcast(bank_m, bank_r, m2row, r2row, row_toks, width=512):
        tm = p.emit("tensor", (lambda: lambda t: mm(
            t, bank_m, pb[bank_m][:, 0:width], ones33.ap(), m2row.ap()[0:33, 0:width],
            True, True))(),
            waits=[row_toks[0]] + ps.wr_waits(bank_m), inc="mm")
        tr = p.emit("tensor", (lambda: lambda t: mm(
            t, bank_r, pb[bank_r][:, 0:width], ones33.ap(), r2row.ap()[0:33, 0:width],
            True, True))(),
            waits=[row_toks[1]] + ps.wr_waits(bank_r), inc="mm")
        return tm, tr

    tz = Mem(nc, az, nc.sbuf_top)
    wkv_b = tz.alloc("wkv_b", [128, NCH * 2048], BF16)
    brow_b = tz.alloc("brow_b", [1, 2048], BF16)
    xb_t = [tz.alloc(f"xbA{i}", [128, NCH * 512], BF16) for i in range(2)]
    sqA = tz.alloc("sqA", [128, NCH * 512], BF16)
    xnt_t = [tz.alloc(f"xntA{i}", [128, NCH * 512], BF16) for i in range(2)]
    fkS = [tz.alloc(f"fkS{i}", [128, 1024], BF16) for i in range(2)]
    fvS = [tz.alloc(f"fvS{i}", [128, 1024], BF16) for i in range(2)]
    fmx = [tz.alloc(f"fmx{i}", [128, 512], BF16) for i in range(2)]
    tfA = tz.alloc("tfA", [128, 512], F32)
    teA = tz.alloc("teA", [128, 512], F32)
    te2A = [tz.alloc(f"te2A{i}", [128, 512], BF16) for i in range(2)]
    m2rA = [tz.alloc(f"m2rA{j}", [33, 512], BF16) for j in range(2)]
    r2rA = [tz.alloc(f"r2rA{j}", [33, 512], BF16) for j in range(2)]
    r1tmp = [tz.alloc(f"r1tmp{j}", [33, 512], BF16) for j in range(2)]
    r1inv_b = [tz.alloc(f"r1inv{j}", [1, 512], BF16) for j in range(2)]
    r1c_sb = [tz.alloc(f"r1c{j}", [128, 8], F32) for j in range(2)]
    rowfA = [tz.alloc(f"rowfA{j}", [1, 512 * 3], F32) for j in range(2)]

    def msA(v):
        last = None
        for j in range(2):
            last = v.memset(m2rA[j].ap(), 0.0)
            last = v.memset(r2rA[j].ap(), 0.0)
            last = v.memset(r1tmp[j].ap(), 0.0)
        last = v.memset(r1store.ap(), 0.0)
        return last
    tok_ms = p.emit("vector", msA, inc="dve")

    tok_wkv = p.dma("gpsimd", lambda g: g.dma_start(
        out=_t3(wkv_b.ap(), 2048), in_=wkvT.rearrange("(a p) o -> p a o", p=128)),
        inc="wgt")
    b1cb = tz.alloc("b1cb", [128, 8], BF16)
    tok_b1c = p.emit("vector", lambda v: v.tensor_copy(b1cb.ap(), b1c.ap()),
                     waits=[tok_cld], inc="dve")
    brow_toks = []
    prev_ev = None
    for g4 in range(4):
        def bmm(t, g4=g4):
            last = None
            for ck in range(NCH):
                last = mm(t, 0, pb[0][0:1, 0:512], b1cb.ap()[:, ck:ck + 1],
                          _t3(wkv_b.ap(), 2048)[:, ck, g4 * 512:(g4 + 1) * 512],
                          ck == 0, ck == NCH - 1)
            return last
        tb_ = p.emit("tensor", bmm, waits=[tok_wkv, tok_b1c, prev_ev], inc="mm")
        prev_ev = p.emit("scalar", (lambda g4: lambda s: s.activation(
            brow_b.ap()[0:1, g4 * 512:(g4 + 1) * 512], pb[0][0:1, 0:512],
            AF.Copy))(g4), waits=[tb_], inc="act")
        brow_toks.append(prev_ev)
    ps.note_read(0, prev_ev)
    def zstart(t):
        last = None
        for b in (5, 6, 7):
            last = t.matmul(pb[b][:, :], c_zr.ap()[0:1, 0:128],
                            indz.ap()[0:1, 0:512], start=True, stop=False)
        return last
    tok_zb = p.emit("tensor", zstart, waits=[tok_cld] + ps.wr_waits([5, 6, 7]),
                    inc="mm")
    for b in (5, 6, 7):
        ps.started[b] = True

    tok_ws = p.emit("vector", lambda v: [v.tensor_scalar_mul(
        _t3(wkv_b.ap(), 2048)[:, ck, :], _t3(wkv_b.ap(), 2048)[:, ck, :],
        g1c.ap()[:, ck:ck + 1]) for ck in range(NCH)][-1],
        waits=brow_toks + [tok_ms], inc="dve")

    xb_free = [None, None]
    xnt_free = [None, None]
    fkv_free = [None, None]
    fmx_free = [None, None]
    te2_free = [None, None]
    rowA_free = [None, None]
    rowf_free = [None, None]
    r1_free = [None, None]
    r1c_free = [None, None]
    r1inv_free = [None, None]
    sq_free = None
    tfA_free = None
    last_kv_tok = None
    last_kvproj_tok = None
    rotA = [0]

    stA = {}

    def st_grp(t, bank, src_sl, width=512):
        last = None
        for c in range(NCH):
            last = mm(t, bank, pb[bank][0:1, 0:width], ones1.ap(), src_sl(c),
                      c == 0, c == NCH - 1)
        return last

    def A_dma(it):
        bp = it % 2
        xb = xb_t[bp]
        stA.setdefault(it, {})["tok_x"] = p.dma(
            "gpsimd", (lambda it, xb: lambda g: g.dma_start(
                out=_t3(xb.ap(), 512),
                in_=xT.rearrange("(a p) t -> p a t", p=128)[:, :, it * 512:(it + 1) * 512]))(it, xb),
            waits=[xb_free[bp], tok_cld], inc="xin")

    def A_P1a(it):
        nonlocal sq_free
        bp = it % 2
        st = stA[it]
        xb = xb_t[bp]
        st["xb_sl"] = (lambda xb: lambda c: _t3(xb.ap(), 512)[:, c, :])(xb)
        st["tok_sq"] = p.emit("gpsimd", (lambda xb: lambda g: g.tensor_tensor(
            sqA.ap(), xb.ap(), xb.ap(), op=ALU.mult))(xb),
            waits=[st["tok_x"], sq_free], inc="pool")
        st["s0"] = p.emit("tensor", (lambda f: lambda t: st_grp(t, 0, f))(st["xb_sl"]),
                          waits=[st["tok_x"]] + ps.wr_waits(0), inc="mm")

    def A_P1b(it):
        nonlocal sq_free
        st = stA[it]
        st["s1"] = p.emit("tensor", (lambda: lambda t: st_grp(t, 1, sq_sl))(),
                          waits=[st["tok_sq"]] + ps.wr_waits(1), inc="mm")
        sq_free = st["s1"]

    def A_P2a(it):
        bp = it % 2
        st = stA[it]
        r0 = ln_rows(0, 1, m2rA[bp], r2rA[bp], rowfA[bp], (st["s0"], st["s1"]),
                     [tok_ms, rowA_free[bp], rowf_free[bp]], preserve_var=True)
        st["r0"] = r0

    def A_P2b(it):
        bp = it % 2
        st = stA[it]
        st["bm"], st["br"] = ln_bcast(0, 1, m2rA[bp], r2rA[bp], st["r0"])

    def A_P3(it):
        nonlocal tfA_free
        own = it < NT_O
        bp = it % 2
        st = stA[it]
        xb = xb_t[bp]
        xn_dst = xn_b if own else xnt_t[bp]
        xn_w = T if own else 512
        xn_off = it * 512 if own else 0
        tok_xn = None
        for c in range(NCH):
            p.emit("vector", (lambda c, xb=xb: lambda v: v.scalar_tensor_tensor(
                tfA.ap(), _t3(xb.ap(), 512)[:, c, :], 1.0, pb[0][:, :],
                op0=ALU.mult, op1=ALU.subtract))(c),
                waits=[st["bm"], st["br"], tfA_free if c == 0 else None,
                       None if own else (xnt_free[bp] if c == 0 else None)])
            tok_xn = p.emit("vector", (lambda c, xn_dst=xn_dst, xn_w=xn_w,
                                       xn_off=xn_off: lambda v: v.tensor_tensor(
                xn_dst.ap()[:, c * xn_w + xn_off: c * xn_w + xn_off + 512],
                tfA.ap(), pb[1][:, :], op=ALU.mult))(c))
        st["tok_xn"] = tok_xn
        xb_free[bp] = tok_xn
        tfA_free = tok_xn
        rowA_free[bp] = tok_xn
        ps.note_read(0, tok_xn)
        ps.note_read(1, tok_xn)
        st["xn_sl"] = (lambda xv, xn_w, xn_off: lambda c:
                       xv[:, c * xn_w + xn_off: c * xn_w + xn_off + 512])(
                       xn_dst.ap(), xn_w, xn_off)
        r1row = r1store if own else r1tmp[bp]
        r1off = it * 512 if own else 0
        mf = rowfA[bp].ap()[0:1, 0:512]
        vf = rowfA[bp].ap()[0:1, 512:1024]
        rf = rowfA[bp].ap()[0:1, 1024:1536]
        p.emit("vector", lambda v: v.tensor_tensor(mf, vf, rf, op=ALU.mult),
               waits=[st["r0"][0], st["r0"][1]], inc="dve")
        tv2 = p.emit("vector", lambda v: v.tensor_tensor(mf, mf, rf, op=ALU.mult),
                     inc="dve")
        ta1 = p.emit("scalar", lambda s: s.activation(mf, mf, AF.Ln, bias=cs(C_EPSLN, 1)),
                     waits=[tv2], inc="act")
        p.emit("scalar", lambda s: s.activation(vf, mf, AF.Exp, scale=cs(C_NEGH, 1)),
               inc="act")
        ta3 = p.emit("scalar", lambda s: s.activation(mf, mf, AF.Exp, scale=cs(C_POSH, 1)),
                     inc="act")
        th1 = p.emit("vector", (lambda r1row, r1off: lambda v: v.tensor_copy(
            r1row.ap()[0:1, r1off:r1off + 512], vf))(r1row, r1off),
            waits=[ta3, r1_free[bp] if not own else None], inc="dve")
        th2 = p.emit("vector", (lambda r1row, r1off: lambda v: v.tensor_tensor(
            r1row.ap()[32:33, r1off:r1off + 512], vf,
            r1row.ap()[0:1, r1off:r1off + 512], op=ALU.subtract))(r1row, r1off),
            inc="dve")
        th3 = p.emit("vector", (lambda bp: lambda v: v.tensor_copy(
            r1inv_b[bp].ap()[0:1, :], mf))(bp), waits=[r1inv_free[bp]], inc="dve")
        st["r1iv"] = th3
        rowf_free[bp] = th3
        st["r1row"] = r1row
        st["r1off"] = r1off
        for sub in range(4):
            bank = 2 + rotA[0] % 3
            rotA[0] += 1
            tcm = p.emit("tensor", (lambda r1row, r1off, sub, bank: lambda t: mm(
                t, bank, pb[bank][:, 0:2],
                r1row.ap()[0:33, r1off + sub * 128: r1off + (sub + 1) * 128],
                c_pm.ap()[0:33, 0:2], True, True))(r1row, r1off, sub, bank),
                waits=[th2] + ps.wr_waits(bank), inc="mm")
            tce = p.emit("scalar", (lambda bp, sub, bank: lambda s: s.activation(
                r1c_sb[bp].ap()[:, sub * 2: sub * 2 + 2], pb[bank][:, 0:2],
                AF.Copy))(bp, sub, bank),
                waits=[tcm, r1c_free[bp] if sub == 0 else None], inc="act")
            ps.note_read(bank, tce)
            st[f"r1c{sub}"] = tce
        r1_free[bp] = st["r1c3"]

    def A_kvproj(it, sub):
        nonlocal last_kvproj_tok
        bp = it % 2
        sp = sub % 2
        st = stA[it]
        fk, fv = fkS[sp], fvS[sp]
        ev = []
        for grp in range(4):
            bank = 2 + rotA[0] % 3
            rotA[0] += 1
            is_k = grp < 2
            def kvmm(t, sub=sub, grp=grp, bank=bank, st=st, bp=bp):
                last = None
                for c in range(NCH):
                    last = mm(t, bank, pb[bank][:, :],
                              st["xn_sl"](c).rearrange("p t -> p t")[:, sub * 128:(sub + 1) * 128]
                              if False else
                              st["xn_sl"](c)[:, sub * 128:(sub + 1) * 128],
                              _t3(wkv_b.ap(), 2048)[:, c, grp * 512:(grp + 1) * 512],
                              c == 0, False)
                last = mm(t, bank, pb[bank][:, :],
                          r1inv_b[bp].ap()[0:1, sub * 128:(sub + 1) * 128],
                          brow_b.ap()[0:1, grp * 512:(grp + 1) * 512],
                          False, True)
                return last
            tok_mm = p.emit("tensor", kvmm,
                            waits=[st["tok_xn"], st[f"r1c{sub}"], st["r1iv"],
                                   tok_ws] + ps.wr_waits(bank), inc="mm")
            go = (grp % 2) * 512
            pcol = r1c_sb[bp].ap()[:, sub * 2: sub * 2 + 1]
            ncol = r1c_sb[bp].ap()[:, sub * 2 + 1: sub * 2 + 2]
            war = fkv_free[sp] if grp in (0, 2) else None
            if is_k:
                gp = grp % 2
                p.emit("scalar", (lambda bank, ncol: lambda s: s.activation(
                    teA.ap(), pb[bank][:, :], AF.Relu, scale=ncol))(bank, ncol),
                    waits=[tok_mm], inc="act")
                t2 = p.emit("scalar", (lambda gp: lambda s: s.activation(
                    te2A[gp].ap(), teA.ap(), AF.Exp, scale=cs(C_NEG1)))(gp),
                    waits=[te2_free[gp]], inc="act")
                t5 = p.emit("scalar", (lambda bank, gp, pcol: lambda s: s.activation(
                    fmx[gp].ap(), pb[bank][:, :], AF.Relu, scale=pcol))(bank, gp, pcol),
                    waits=[fmx_free[gp]], inc="act")
                t4 = p.emit("gpsimd", (lambda go, gp, fk=fk: lambda g: g.tensor_tensor(
                    fk.ap()[:, go:go + 512], fmx[gp].ap(), te2A[gp].ap(),
                    op=ALU.add))(go, gp),
                    waits=[t2, t5, war], inc="pool")
                te2_free[gp] = t4
                fmx_free[gp] = t4
                ps.note_read(bank, t5)
                ev.append(t4)
            else:
                t3 = p.emit("scalar", (lambda bank, go, pcol, fv=fv: lambda s:
                            s.activation(fv.ap()[:, go:go + 512], pb[bank][:, :],
                                         AF.Identity, scale=pcol))(bank, go, pcol),
                    waits=[tok_mm, war], inc="act")
                ps.note_read(bank, t3)
                ev.append(t3)
            last_kvproj_tok = tok_mm
        st[f"ev{sub}"] = ev

    def A_kvacc(it, sub):
        nonlocal last_kv_tok
        sp = sub % 2
        st = stA[it]
        fk, fv = fkS[sp], fvS[sp]
        final = (it == NT_A - 1 and sub == 3)
        def kv_acc(t, fk=fk, fv=fv, final=final):
            last = None
            for pr in range(8):
                bank = 6 if pr < 4 else 7
                last = mm(t, bank, pb[bank][:, (pr % 4) * 128:(pr % 4) * 128 + 128],
                          fk.ap()[:, pr * 128:(pr + 1) * 128],
                          fv.ap()[:, pr * 128:(pr + 1) * 128],
                          False, final and pr % 4 == 3)
            for pr in range(8):
                last = mm(t, 5, pb[5][:, pr:pr + 1],
                          fk.ap()[:, pr * 128:(pr + 1) * 128],
                          ones1.ap(), False, final and pr == 7)
            return last
        last_kv_tok = p.emit("tensor", kv_acc, waits=st[f"ev{sub}"] + [tok_zb],
                             inc="mm")
        fkv_free[sp] = last_kv_tok

    sq_sl = lambda c: _t3(sqA.ap(), 512)[:, c, :]
    pending_acc = None
    for it in range(NT_A):
        if it == 0:
            A_dma(0); A_dma(1)
            A_P1a(0); A_P1b(0); A_P2a(0); A_P2b(0); A_P3(0)
        if it + 1 < NT_A:
            if it + 2 < NT_A:
                A_dma(it + 2)
            A_P1a(it + 1)
        A_kvproj(it, 0)
        if pending_acc is not None:
            A_kvacc(*pending_acc)
        if it + 1 < NT_A:
            A_P1b(it + 1)
            A_P2a(it + 1)
        A_kvproj(it, 1)
        A_kvacc(it, 0)
        if it + 1 < NT_A:
            A_P2b(it + 1)
        A_kvproj(it, 2)
        A_kvacc(it, 1)
        if it + 1 < NT_A:
            A_P3(it + 1)
        A_kvproj(it, 3)
        r1c_free[it % 2] = p.last("act")
        r1inv_free[it % 2] = last_kvproj_tok
        A_kvacc(it, 2)
        pending_acc = (it, 3)
    A_kvacc(*pending_acc)

    if "t_xn" in tap_out:
        p.dma("sync", lambda e: e.dma_start(out=tap_out["t_xn"], in_=xn_b.ap()),
              waits=[p.last("dve")], inc="tap")

    tok_z1 = p.emit("gpsimd", lambda g: (g.memset(kvblk.ap(), 0.0),
                                         g.memset(ksmat.ap(), 0.0))[-1],
                    waits=[tok_cld], inc="pool")
    kvb_last = None
    for pr in range(8):
        bank = 6 if pr < 4 else 7
        o = (pr % 4) * 128
        kvb_last = p.emit("vector", (lambda bank, o, pr: lambda v: (
            v.tensor_copy(kvblk.ap()[0:64, pr * 128: pr * 128 + 64],
                          pb[bank][0:64, o:o + 64]),
            v.tensor_copy(kvblk.ap()[64:128, pr * 128 + 64: pr * 128 + 128],
                          pb[bank][64:128, o + 64:o + 128]))[-1])(bank, o, pr),
            waits=[last_kv_tok, tok_z1], inc="dve")
    for c in range(NCH):
        kvb_last = p.emit("vector", (lambda c: lambda v: (
            v.tensor_copy(ksmat.ap()[0:64, c * 16 + 2 * c: c * 16 + 2 * c + 1],
                          pb[5][0:64, c:c + 1]),
            v.tensor_copy(ksmat.ap()[64:128, c * 16 + 2 * c + 1: c * 16 + 2 * c + 2],
                          pb[5][64:128, c:c + 1]))[-1])(c),
            waits=[last_kv_tok, tok_z1], inc="dve")
    tok_kvb = kvb_last
    for b in (5, 6, 7):
        ps.started[b] = False
        ps.note_read(b, tok_kvb)
    if "t_kvb" in tap_out:
        p.dma("sync", lambda e: e.dma_start(out=tap_out["t_kvb"], in_=kvblk.ap()),
              waits=[tok_kvb], inc="tap")
    if "t_ksm" in tap_out:
        p.dma("sync", lambda e: e.dma_start(out=tap_out["t_ksm"], in_=ksmat.ap()),
              waits=[tok_kvb], inc="tap")
    if stop_after == "A":
        p.emit("sync", lambda e: None, waits=p.stream_last("tap") + [tok_kvb])
        with nc.Block() as block:
            sems = p.make_sems(nc, stack)
            p.build(block, sems)
        stack.close()
        return nc

    wzB = Mem(nc, az, nc.sbuf_top)
    wq_b = wzB.alloc("wq_b", [128, NCH * 1024], BF16)
    pj_b = wzB.alloc("pj_b", [128, NCH * 1024], BF16)
    tok_wq = p.dma("gpsimd", lambda g: g.dma_start(
        out=_t3(wq_b.ap(), 1024), in_=wqT.rearrange("(a p) o -> p a o", p=128)),
        waits=[last_kvproj_tok], inc="wgt")
    tok_pj = p.dma("gpsimd", lambda g: g.dma_start(
        out=_t3(pj_b.ap(), 1024), in_=projT.rearrange("(a p) o -> p a o", p=128)),
        inc="wgt")
    tok_rw = p.dma("gpsimd", lambda g: g.dma_start(
        out=_t3(rw_b.ap(), 16), in_=rw.rearrange("(a p) e -> p a e", p=128)),
        inc="wgt")

    tzB = wzB
    h1B = [tzB.alloc(f"h1B{i}", [128, NCH * 512], BF16) for i in range(2)]
    fqB = tzB.alloc("fqB", [128, NCH * 512], BF16)
    atB = tzB.alloc("atB", [128, NCH * 512], BF16)
    tqB = [tzB.alloc(f"tqB{i}", [128, 512], F32) for i in range(2)]
    y1B = [tzB.alloc(f"y1B{i}", [128, NCH * 512], BF16) for i in range(2)]
    z2B = tzB.alloc("z2B", [48, 512], BF16)
    zfB = tzB.alloc("zfB", [16, 512], F32)
    tf2B = [tzB.alloc(f"tf2B{i}", [128, 512], F32) for i in range(2)]
    teB = tzB.alloc("teB", [128, 512], F32)
    te2B = [tzB.alloc(f"te2B{i}", [128, 512], BF16) for i in range(2)]
    tpB = [tzB.alloc(f"tpB{i}", [128, 512], F32) for i in range(2)]

    tok_msB = p.emit("vector", lambda v: v.memset(z2B.ap(), 0.0),
                     waits=[p.last("dve")], inc="dve")

    h1B_free = [None, None]
    y1B_free = [None, None]
    tf2B_free = [None, None]
    te2B_free = [None, None]
    tpB_free = [None, None]
    z2_free = None
    at_free = None
    tq_free = [None, None]
    stB = {}

    def st_grpB(t, bank, src_sl):
        last = None
        for c in range(NCH):
            last = mm(t, bank, pb[bank][0:1, 0:512], ones1.ap(), src_sl(c),
                      c == 0, c == NCH - 1)
        return last

    def B_S1(it):
        st = stB[it] = {}
        st["xn_sl"] = (lambda it: lambda c:
                       _t3(xn_b.ap(), T)[:, c, it * 512:(it + 1) * 512])(it)
        br1 = p.emit("tensor", (lambda it: lambda t: mm(
            t, 0, pb[0][:, :], ones33.ap(),
            r1store.ap()[0:33, it * 512:(it + 1) * 512], True, True))(it),
            waits=[tok_kvb] + ps.wr_waits(0), inc="mm")
        bp = it % 2
        h1 = h1B[bp]
        prev_ta = None
        for c in range(NCH):
            cp = c % 2
            td = p.emit("vector", (lambda c, cp, xn_sl=st["xn_sl"]: lambda v:
                        v.tensor_tensor(tf2B[cp].ap(), xn_sl(c), pb[0][:, :],
                                        op=ALU.mult))(c, cp),
                waits=[br1, tf2B_free[cp],
                       h1B_free[bp] if c == 0 else None], inc="dve")
            prev_ta = p.emit("scalar", (lambda c, cp, h1=h1: lambda s: s.activation(
                _t3(h1.ap(), 512)[:, c, :], tf2B[cp].ap(), AF.Identity,
                scale=g1c.ap()[:, c:c + 1], bias=b1c.ap()[:, c:c + 1]))(c, cp),
                waits=[td], inc="act")
            tf2B_free[cp] = prev_ta
        st["tok_h1"] = prev_ta
        ps.note_read(0, p.last("dve"))

    def B_S2(it):
        bp = it % 2
        st = stB[it]
        h1 = h1B[bp]
        fq_last = None
        for o in range(NCH):
            bank = 2 + o % 2
            def qmm(t, o=o, bank=bank, h1=h1):
                last = None
                for c in range(NCH):
                    last = mm(t, bank, pb[bank][:, :],
                              _t3(wq_b.ap(), 1024)[:, c, o * 128:(o + 1) * 128],
                              _t3(h1.ap(), 512)[:, c, :], c == 0, c == NCH - 1)
                return last
            tq = p.emit("tensor", qmm, waits=[st["tok_h1"], tok_wq] + ps.wr_waits(bank),
                        inc="mm")
            gp = o % 2
            p.emit("scalar", (lambda bank: lambda s: s.activation(
                teB.ap(), pb[bank][:, :], AF.Relu, scale=cs(C_NEG1)))(bank),
                waits=[tq], inc="act")
            t2 = p.emit("scalar", (lambda gp: lambda s: s.activation(
                te2B[gp].ap(), teB.ap(), AF.Exp, scale=cs(C_NEG1)))(gp),
                waits=[te2B_free[gp]], inc="act")
            t3 = p.emit("vector", (lambda bank, o, gp: lambda v: v.scalar_tensor_tensor(
                _t3(fqB.ap(), 512)[:, o, :], pb[bank][:, :], 0.0, te2B[gp].ap(),
                op0=ALU.max, op1=ALU.add))(bank, o, gp),
                waits=[t2], inc="dve")
            te2B_free[gp] = t3
            ps.note_read(bank, t3)
            fq_last = t3
        st["fq_last"] = fq_last
        h1B_free[bp] = fq_last
        if "t_fq" in tap_out and it == 0:
            p.dma("sync", lambda e: e.dma_start(out=tap_out["t_fq"], in_=fqB.ap()),
                  waits=[fq_last], inc="tap")

    def B_S3a(it):
        nonlocal z2_free
        st = stB[it]
        def dmm(t):
            last = None
            for c in range(NCH):
                last = mm(t, 4, pb[4][0:16, 0:512],
                          ksmat.ap()[:, c * 16:(c + 1) * 16],
                          _t3(fqB.ap(), 512)[:, c, :], c == 0, c == NCH - 1)
            return last
        tok_dn = p.emit("tensor", dmm,
                        waits=[st["fq_last"], tok_kvb] + ps.wr_waits(4), inc="mm")
        p.emit("scalar", lambda s: s.activation(zfB.ap(), pb[4][0:16, 0:512],
                                                AF.Ln, bias=cs(C_EPSATTN, 16)),
               waits=[tok_dn], inc="act")
        tz2 = p.emit("scalar", lambda s: s.activation(zfB.ap(), zfB.ap(), AF.Exp,
                                                      scale=cs(C_NEG1, 16)), inc="act")
        ps.note_read(4, tz2)
        p.emit("vector", lambda v: v.tensor_copy(z2B.ap()[0:16, :], zfB.ap()),
               waits=[tz2, z2_free], inc="dve")
        st["tz4"] = p.emit("vector", lambda v: v.tensor_tensor(
            z2B.ap()[32:48, :], zfB.ap(), z2B.ap()[0:16, :], op=ALU.subtract),
            inc="dve")

    def B_S3b(it):
        nonlocal z2_free, at_free
        st = stB[it]
        at_last = None
        zb_last = None
        for c in range(NCH):
            bz = 5 + c % 2
            tb = p.emit("tensor", (lambda c, bz: lambda t: mm(
                t, bz, pb[bz][:, :], indz.ap()[:, c * 128:(c + 1) * 128],
                z2B.ap()[0:48, :], True, True))(c, bz),
                waits=[st["tz4"]] + ps.wr_waits(bz), inc="mm")
            zb_last = tb
            ba = 7 if c % 2 == 0 else 4
            ta = p.emit("tensor", (lambda c, ba: lambda t: mm(
                t, ba, pb[ba][:, :], kvblk.ap()[:, c * 128:(c + 1) * 128],
                _t3(fqB.ap(), 512)[:, c, :], True, True))(c, ba),
                waits=[st["fq_last"]] + ps.wr_waits(ba), inc="mm")
            cp = c % 2
            tqa = p.emit("scalar", (lambda c, ba, cp: lambda s: s.activation(
                tqB[cp].ap(), pb[ba][:, :], AF.Copy))(c, ba, cp),
                waits=[ta, tq_free[cp]], inc="act")
            ps.note_read(ba, tqa)
            tat = p.emit("vector", (lambda c, bz, cp: lambda v: v.tensor_tensor(
                _t3(atB.ap(), 512)[:, c, :], tqB[cp].ap(), pb[bz][:, :],
                op=ALU.mult))(c, bz, cp),
                waits=[tqa, tb, at_free if c == 0 else None], inc="dve")
            ps.note_read(bz, tat)
            tq_free[cp] = tat
            at_last = tat
        z2_free = zb_last
        st["at_last"] = at_last

    def B_S4(it):
        bp = it % 2
        st = stB[it]
        y1 = y1B[bp]
        y1_toks = []
        for o in range(NCH):
            bank = 5 + o % 2
            opar = o % 2
            def pmm(t, o=o, bank=bank):
                last = None
                for c in range(NCH):
                    last = mm(t, bank, pb[bank][:, :],
                              _t3(pj_b.ap(), 1024)[:, c, o * 128:(o + 1) * 128],
                              _t3(atB.ap(), 512)[:, c, :], c == 0, c == NCH - 1)
                return last
            tp_ = p.emit("tensor", pmm,
                         waits=[st["at_last"], tok_pj] + ps.wr_waits(bank), inc="mm")
            t1 = p.emit("scalar", (lambda o, bank, opar: lambda s: s.activation(
                tpB[opar].ap(), pb[bank][:, :], AF.Identity,
                bias=pjc.ap()[:, o:o + 1]))(o, bank, opar),
                waits=[tp_, tpB_free[opar]], inc="act")
            ps.note_read(bank, t1)
            t2 = p.emit("gpsimd", (lambda o, opar, y1, it=it: lambda g: g.tensor_tensor(
                _t3(y1.ap(), 512)[:, o, :], tpB[opar].ap(),
                _t3(xn_b.ap(), T)[:, o, it * 512:(it + 1) * 512], op=ALU.add))(o, opar, y1),
                waits=[t1, y1B_free[bp] if o == 0 else None], inc="pool")
            tpB_free[opar] = t2
            y1_toks.append(t2)
        tok_sp = p.dma("sync", (lambda it, y1: lambda e: e.dma_start(
            out=_t3(y1sp, T)[:, :, it * 512:(it + 1) * 512],
            in_=_t3(y1.ap(), 512)))(it, y1), waits=y1_toks, inc="spl")
        y1B_free[bp] = tok_sp
        return p.last("dve")

    B_S1(0); B_S2(0)
    for it in range(NT_O):
        B_S3a(it)
        if it + 1 < NT_O:
            B_S1(it + 1)
        B_S3b(it)
        B_S4(it)
        if it + 1 < NT_O:
            B_S2(it + 1)

    if stop_after == "B1":
        p.emit("sync", lambda e: None, waits=p.stream_last("spl") + p.stream_last("tap"))
        with nc.Block() as block:
            sems = p.make_sems(nc, stack)
            p.build(block, sems)
        stack.close()
        return nc

    prr = tzB.alloc("prr", [128, 64], F32)
    msc = tzB.alloc("msc", [128, 64], F32)
    red4 = tzB.alloc("red4", [128, 16], F32)
    ctok = tzB.alloc("ctok", [128, 64], F32)
    l3 = lambda apv: apv.rearrange("p (a e) -> p a e", e=16)
    r3 = lambda col: red4.ap()[:, col * 4:(col + 1) * 4].rearrange("p (a e) -> p a e", e=1)
    r3b = lambda col: r3(col).broadcast_to((128, 4, 16))

    for it in range(NT_O):
        def lg_mm(t, it=it):
            last = None
            for s4 in range(4):
                for c in range(NCH):
                    last = mm(t, 0, pb[0][:, s4 * 16:s4 * 16 + 16],
                              _t3(xn_b.ap(), T)[:, c,
                                                it * 512 + s4 * 128: it * 512 + (s4 + 1) * 128],
                              _t3(rw_b.ap(), 16)[:, c, :],
                              c == 0, c == NCH - 1)
            return last
        tok_lg = p.emit("tensor", lg_mm, waits=[tok_rw] + ps.wr_waits(0), inc="mm")
        pb0v = pb[0].ap()[:, 0:64].rearrange("p (a e) -> p a e", e=16)
        t1 = p.emit("vector", lambda v: v.reduce_max(r3(0), pb0v, axis=AX.X, negate=True),
                    waits=[tok_lg], inc="dve")
        t2 = p.emit("vector", lambda v: v.tensor_tensor(l3(msc.ap()), pb0v, r3b(0),
                                                        op=ALU.add),
                    waits=[t1], inc="dve")
        ps.note_read(0, t2)
        t3 = p.emit("scalar", lambda s: s.activation(msc.ap(), msc.ap(), AF.Exp),
                    waits=[t2], inc="act")
        t4 = p.emit("vector", lambda v: v.reduce_sum(r3(1), l3(msc.ap()), axis=AX.X),
                    waits=[t3], inc="dve")
        p.emit("scalar", lambda s: s.activation(red4.ap()[:, 4:8], red4.ap()[:, 4:8],
                                                AF.Ln), waits=[t4], inc="act")
        t6 = p.emit("scalar", lambda s: s.activation(red4.ap()[:, 4:8], red4.ap()[:, 4:8],
                                                     AF.Exp, scale=cs(C_NEG1)), inc="act")
        t7 = p.emit("vector", lambda v: v.tensor_tensor(l3(prr.ap()), l3(msc.ap()),
                                                        r3b(1), op=ALU.mult),
                    waits=[t6], inc="dve")
        t8 = p.emit("vector", lambda v: v.reduce_max(r3(2), l3(prr.ap()), axis=AX.X),
                    inc="dve")
        t9 = p.emit("vector", lambda v: v.tensor_tensor(l3(msc.ap()), l3(prr.ap()),
                                                        r3b(2), op=ALU.is_equal),
                    waits=[t8], inc="dve")
        p.emit("vector", lambda v: v.scalar_tensor_tensor(
            l3(msc.ap()), l3(msc.ap()), -1e30, l3(prr.ap()), op0=ALU.mult, op1=ALU.add),
            inc="dve")
        t11 = p.emit("vector", lambda v: v.reduce_max(r3(3), l3(msc.ap()), axis=AX.X),
                     inc="dve")
        t12 = p.emit("vector", lambda v: v.tensor_tensor(l3(msc.ap()), l3(prr.ap()),
                                                         r3b(3), op=ALU.is_ge),
                     waits=[t11], inc="dve")
        t13 = p.emit("vector", lambda v: v.tensor_tensor(
            red4.ap()[:, 12:16], red4.ap()[:, 8:12], red4.ap()[:, 12:16], op=ALU.add),
            inc="dve")
        p.emit("scalar", lambda s: s.activation(red4.ap()[:, 12:16], red4.ap()[:, 12:16],
                                                AF.Ln, bias=cs(C_EPSRTR)),
               waits=[t13], inc="act")
        t15 = p.emit("scalar", lambda s: s.activation(red4.ap()[:, 12:16],
                                                      red4.ap()[:, 12:16], AF.Exp,
                                                      scale=cs(C_NEG1)), inc="act")
        t16 = p.emit("vector", lambda v: v.tensor_tensor(l3(ctok.ap()), l3(prr.ap()),
                                                         l3(msc.ap()), op=ALU.mult),
                     waits=[t12, t15], inc="dve")
        t17 = p.emit("vector", lambda v: v.tensor_tensor(l3(ctok.ap()), l3(ctok.ap()),
                                                         r3b(3), op=ALU.mult),
                     inc="dve")
        for s4 in range(4):
            btr = 1 + s4 % 2
            ttr = p.emit("tensor", (lambda s4, btr: lambda t: t.transpose(
                pb[btr][0:16, 0:128], ctok.ap()[:, s4 * 16:(s4 + 1) * 16],
                ident.ap()))(s4, btr),
                waits=[t17] + ps.wr_waits(btr), inc="mm")
            tcp = p.emit("vector", (lambda s4, btr, it=it: lambda v: v.tensor_copy(
                comb_fm.ap()[0:16, it * 512 + s4 * 128: it * 512 + (s4 + 1) * 128],
                pb[btr][0:16, 0:128]))(s4, btr),
                waits=[ttr], inc="dve")
            ps.note_read(btr, tcp)
    tok_comb = p.last("dve")
    if "t_comb" in tap_out:
        p.dma("sync", lambda e: e.dma_start(out=tap_out["t_comb"], in_=comb_fm.ap()),
              waits=[tok_comb], inc="tap")

    tzC = Mem(nc, az, nc.sbuf_top)
    midC = tzC.alloc("midC", [128, 40 * 1024], BF16)
    gC = tzC.alloc("gC", [128, 4 * 1024], BF16)
    siC = [tzC.alloc(f"siC{i}", [128, 512], BF16) for i in range(2)]
    ttC = [tzC.alloc(f"ttC{i}", [128, 512], BF16) for i in range(2)]
    wgb = [tzC.alloc(f"wgb{i}", [128, NCH * 256], BF16) for i in range(2)]
    wub = [tzC.alloc(f"wub{i}", [128, NCH * 256], BF16) for i in range(2)]
    wdb = [tzC.alloc(f"wdb{i}", [128, 2 * 1024], BF16) for i in range(2)]

    C_GATE = p.stream_last("spl")
    wgu_free = [None, None]
    wd_free = [None, None]
    si_free = [None, None]
    tt_free = [None, None]
    mid_free = None
    gC_free = None
    wgu_tok = {}
    wd_tok = {}

    def wload_gu(q):
        it, e = divmod(q, NEX)
        bpw = q % 2
        t1 = p.dma("gpsimd", (lambda e, bpw: lambda g: g.dma_start(
            out=_t3(wgb[bpw].ap(), 256),
            in_=wg[e].rearrange("(a p) m -> p a m", p=128)))(e, bpw),
            waits=[wgu_free[bpw], tok_comb if q < 2 else None,
                   tok_wq if q == 0 else None,
                   tok_pj if q == 0 else None] +
                  (C_GATE if q < 2 else []), inc="wgt")
        t2 = p.dma("gpsimd", (lambda e, bpw: lambda g: g.dma_start(
            out=_t3(wub[bpw].ap(), 256),
            in_=wu[e].rearrange("(a p) m -> p a m", p=128)))(e, bpw), inc="wgt")
        wgu_tok[q] = (t1, t2)

    def wload_d(q):
        it_half, e = divmod(q, NEX)
        bpw = q % 2
        wd_tok[q] = p.dma("gpsimd", (lambda e, bpw: lambda g: g.dma_start(
            out=_t3(wdb[bpw].ap(), 1024),
            in_=wd[e].rearrange("(a p) o -> p a o", p=128)))(e, bpw),
            waits=[wd_free[bpw], tok_comb if q < 2 else None] +
                  (C_GATE if q < 2 else []), inc="wgt")

    wave = 0
    for it in range(NT_C):
        mid_toks = []
        for e in range(NEX):
            q = it * NEX + e
            if q == 0:
                wload_gu(0)
                wload_gu(1)
            routed = e < 16
            for sub in range(2):
                t0 = it * 1024 + sub * 512
                if routed:
                    bc = 6 + (e * 2 + sub) % 2
                    tkb = p.emit("tensor", (lambda e, bc, t0: lambda t: mm(
                        t, bc, pb[bc][:, :], inde.ap()[:, e * 128:(e + 1) * 128],
                        comb_fm.ap()[0:16, t0:t0 + 512], True, True))(e, bc, t0),
                        waits=[tok_comb] + ps.wr_waits(bc), inc="mm")
                for mc in range(2):
                    w = wave % 2
                    wave += 1
                    bg, bu = 0 + w, 2 + w
                    def gmm(t, e=e, mc=mc, bg=bg, q=q, t0=t0):
                        last = None
                        for c in range(NCH):
                            last = mm(t, bg, pb[bg][:, :],
                                      _t3(wgb[q % 2].ap(), 256)[:, c, mc * 128:(mc + 1) * 128],
                                      _t3(xn_b.ap(), T)[:, c, t0:t0 + 512],
                                      c == 0, c == NCH - 1)
                        return last
                    def umm(t, e=e, mc=mc, bu=bu, q=q, t0=t0):
                        last = None
                        for c in range(NCH):
                            last = mm(t, bu, pb[bu][:, :],
                                      _t3(wub[q % 2].ap(), 256)[:, c, mc * 128:(mc + 1) * 128],
                                      _t3(xn_b.ap(), T)[:, c, t0:t0 + 512],
                                      c == 0, c == NCH - 1)
                        return last
                    tg = p.emit("tensor", gmm,
                                waits=[wgu_tok[q][0]] + ps.wr_waits(bg), inc="mm")
                    tu = p.emit("tensor", umm,
                                waits=[wgu_tok[q][1]] + ps.wr_waits(bu), inc="mm")
                    ts1 = p.emit("scalar", (lambda bg, w: lambda s: s.activation(
                        siC[w].ap(), pb[bg][:, :], AF.Silu))(bg, w),
                        waits=[tg, si_free[w]] +
                              (C_GATE if it == 0 and e == 0 and sub == 0 and mc == 0 else []),
                        inc="act")
                    ps.note_read(bg, ts1)
                    mslot = e * 2 + mc
                    moff = mslot * 1024 + sub * 512
                    if routed:
                        ts2 = p.emit("vector", (lambda bu, w: lambda v: v.tensor_tensor(
                            ttC[w].ap(), pb[bu][:, :], siC[w].ap(), op=ALU.mult))(bu, w),
                            waits=[ts1, tu, tt_free[w]], inc="dve")
                        ps.note_read(bu, ts2)
                        ts3 = p.emit("vector", (lambda moff, bc, w: lambda v:
                                     v.tensor_tensor(
                                         midC.ap()[:, moff:moff + 512], ttC[w].ap(),
                                         pb[bc][:, :], op=ALU.mult))(moff, bc, w),
                            waits=[tkb, mid_free if e == 0 and sub == 0 and mc == 0 else None] +
                                  (C_GATE if it == 0 and e == 0 and sub == 0 and mc == 0 else []),
                            inc="dve")
                        ps.note_read(bc, ts3)
                        tt_free[w] = ts3
                    else:
                        ts3 = p.emit("vector", (lambda moff, bu, w: lambda v:
                                     v.tensor_tensor(
                                         midC.ap()[:, moff:moff + 512], pb[bu][:, :],
                                         siC[w].ap(), op=ALU.mult))(moff, bu, w),
                            waits=[ts1, tu], inc="dve")
                        ps.note_read(bu, ts3)
                    si_free[w] = ts3
                    mid_toks.append(ts3)
            if q + 2 < NT_C * NEX:
                wgu_free[q % 2] = mid_toks[-1]
                wload_gu(q + 2)
        g_toks = []
        for half in range(2):
            qd0 = (it * 2 + half) * NEX
            if half == 0 and it == 0:
                wload_d(0)
                wload_d(1)
            for e in range(NEX):
                qd = qd0 + e
                def dn_e(t, e=e, half=half, qd=qd, it=it):
                    last = None
                    for mc in range(2):
                        kc = e * 2 + mc
                        for o4 in range(4):
                            o = half * 4 + o4
                            for sub in range(2):
                                bank = o4 * 2 + sub
                                last = mm(t, bank, pb[bank][:, :],
                                          _t3(wdb[qd % 2].ap(), 1024)[:, mc, o * 128:(o + 1) * 128],
                                          midC.ap()[:, kc * 1024 + sub * 512: kc * 1024 + sub * 512 + 512],
                                          e == 0 and mc == 0, e == NEX - 1 and mc == 1)
                        if False:
                            pass
                    return last
                wts = [wd_tok[qd]]
                if e == 0:
                    wts += mid_toks + [w for b in range(8) for w in ps.wr_waits(b)]
                tdn = p.emit("tensor", dn_e, waits=wts, inc="mm")
                wd_free[qd % 2] = tdn
                if qd + 2 < NT_C * 2 * NEX:
                    wload_d(qd + 2)
            half_toks = []
            for o4 in range(4):
                for sub in range(2):
                    bank = o4 * 2 + sub
                    tsg = p.emit("scalar", (lambda o4, sub, bank: lambda s:
                                 s.activation(
                                     gC.ap()[:, o4 * 1024 + sub * 512: o4 * 1024 + sub * 512 + 512],
                                     pb[bank][:, :], AF.Silu))(o4, sub, bank),
                        waits=[tdn, gC_free if (o4 == 0 and sub == 0) else None] +
                              (C_GATE if it == 0 and half == 0 and o4 == 0 and sub == 0 else []),
                        inc="act")
                    ps.note_read(bank, tsg)
                    half_toks.append(tsg)
            g_toks += half_toks
            gC_free = p.dma("sync", (lambda it, half: lambda e: e.dma_start(
                out=_t3(gsp, T)[:, half * 4: half * 4 + 4,
                                it * 1024:(it + 1) * 1024],
                in_=_t3(gC.ap(), 1024)))(it, half),
                waits=half_toks, inc="spl")
        mid_free = g_toks[-1]

    if stop_after == "C":
        p.emit("sync", lambda e: None, waits=p.stream_last("spl") + p.stream_last("tap"))
        with nc.Block() as block:
            sems = p.make_sems(nc, stack)
            p.build(block, sems)
        stack.close()
        return nc

    tzD = Mem(nc, b2_base, nc.sbuf_top)
    fc1_b = tzD.alloc("fc1_b", [128, NCH * HID], BF16)
    fc2_b = tzD.alloc("fc2_b", [128, 32 * 1024], BF16)
    y1t = [tzD.alloc(f"y1t{i}", [128, NCH * 256], BF16) for i in range(2)]
    gt = [tzD.alloc(f"gt{i}", [128, NCH * 256], BF16) for i in range(2)]
    xf2 = [tzD.alloc(f"xf2{i}", [128, NCH * 256], F32) for i in range(2)]
    geluD = tzD.alloc("geluD", [128, 32 * 256], BF16)
    h2D = tzD.alloc("h2D", [128, NCH * 256], BF16)
    sqD = tzD.alloc("sqD", [128, NCH * 256], BF16)
    sfD = [tzD.alloc(f"sfD{i}", [128, 256], F32) for i in range(2)]
    tfD = tzD.alloc("tfD", [128, 256], F32)
    tf2D = [tzD.alloc(f"tf2D{i}", [128, 256], F32) for i in range(2)]
    m2D = [tzD.alloc("m2D0", [33, 256], BF16)] * 2
    r2D = [tzD.alloc("r2D0", [33, 256], BF16)] * 2
    rowfD = [tzD.alloc("rowfD0", [1, 256 * 3], F32)] * 2

    tok_cdone = p.last("mm")
    tok_f1 = p.dma("gpsimd", lambda g: g.dma_start(
        out=_t3(fc1_b.ap(), HID), in_=fc1T.rearrange("(a p) o -> p a o", p=128)),
        waits=[tok_cdone, p.last("act"), p.last("dve")] + p.stream_last("spl") +
              p.stream_last("tap"), inc="wgt")
    tok_f2 = p.dma("gpsimd", lambda g: g.dma_start(
        out=_t3(fc2_b.ap(), 1024), in_=fc2T.rearrange("(a p) o -> p a o", p=128)),
        inc="wgt")
    tok_msD = p.emit("vector", lambda v: (
        v.memset(m2D[0].ap(), 0.0), v.memset(r2D[0].ap(), 0.0))[-1],
        waits=[tok_f1], inc="dve")

    all_spl = p.stream_last("spl")
    y1t_free = [None, None]
    gt_free = [None, None]
    xf2_free = [None, None]
    rowD_free = [None]
    sqD_free = tok_f1
    tfD_free = None
    tf2D_free = [None, None]
    sfD_free = [None, None]
    h2_free = None
    gelu_free = None
    out_toks = []

    stD = {}

    def st_grpD(t, bank, src_sl):
        last = None
        for c in range(NCH):
            last = mm(t, bank, pb[bank][0:1, 0:256], ones1.ap(), src_sl(c),
                      c == 0, c == NCH - 1)
        return last

    def D_S1(it):
        nonlocal sqD_free, tfD_free, h2_free
        bp = it % 2
        st = stD[it] = {}
        ty1 = p.dma("sync", (lambda it, bp: lambda e: e.dma_start(
            out=_t3(y1t[bp].ap(), 256),
            in_=_t3(y1sp, T)[:, :, it * 256:(it + 1) * 256]))(it, bp),
            waits=all_spl + [tok_f1, y1t_free[bp]], inc="y1g")
        st["tg_"] = p.dma("sync", (lambda it, bp: lambda e: e.dma_start(
            out=_t3(gt[bp].ap(), 256),
            in_=_t3(gsp, T)[:, :, it * 256:(it + 1) * 256]))(it, bp),
            waits=[gt_free[bp]], inc="y1g")
        st["tx_"] = p.dma("sync", (lambda it, bp: lambda e: e.dma_start(
            out=_t3(xf2[bp].ap(), 256),
            in_=xT.rearrange("(a p) t -> p a t", p=128)[:, :, it * 256:(it + 1) * 256]))(it, bp),
            waits=[xf2_free[bp]], inc="xb2")
        y1_sl = (lambda bp: lambda c: _t3(y1t[bp].ap(), 256)[:, c, :])(bp)
        st["y1_sl"] = y1_sl
        tok_sq = p.emit("gpsimd", (lambda y1_sl: lambda g: [
            g.tensor_tensor(sq_slD(c), y1_sl(c), y1_sl(c), op=ALU.mult)
            for c in range(NCH)][-1])(y1_sl),
            waits=[ty1, sqD_free], inc="pool")
        sA2 = p.emit("tensor", (lambda f: lambda t: st_grpD(t, 0, f))(y1_sl),
                     waits=[ty1, tok_f1] + ps.wr_waits(0), inc="mm")
        sB2 = p.emit("tensor", (lambda f: lambda t: st_grpD(t, 1, f))(sq_slD),
                     waits=[tok_sq] + ps.wr_waits(1), inc="mm")
        sqD_free = sB2
        r2_ = ln_rows(0, 1, m2D[0], r2D[0], rowfD[0], (sA2, sB2),
                      [tok_msD, rowD_free[0]], width=256)
        bm2, br2 = ln_bcast(0, 1, m2D[0], r2D[0], r2_, width=256)
        prev_ta = None
        for c in range(NCH):
            cp = c % 2
            p.emit("vector", (lambda c, y1_sl=y1_sl: lambda v:
                v.scalar_tensor_tensor(tfD.ap(), y1_sl(c), 1.0, pb[0][:, 0:256],
                                       op0=ALU.mult, op1=ALU.subtract))(c),
                waits=[bm2, br2, tfD_free if c == 0 else None])
            td = p.emit("vector", (lambda c, cp: lambda v: v.tensor_tensor(
                tf2D[cp].ap(), tfD.ap(), pb[1][:, 0:256], op=ALU.mult))(c, cp),
                waits=[tf2D_free[cp], h2_free if c == 0 else None])
            prev_ta = p.emit("scalar", (lambda c, cp: lambda s: s.activation(
                _t3(h2D.ap(), 256)[:, c, :], tf2D[cp].ap(), AF.Identity,
                scale=g2c.ap()[:, c:c + 1], bias=b2c.ap()[:, c:c + 1]))(c, cp),
                waits=[td], inc="act")
            tf2D_free[cp] = prev_ta
        st["tok_h2"] = prev_ta
        tfD_free = p.last("dve")
        rowD_free[0] = p.last("dve")
        ps.note_read(0, p.last("dve"))
        ps.note_read(1, p.last("dve"))

    def D_S2(it):
        nonlocal h2_free
        st = stD[it]
        gl_last = None
        for o in range(32):
            bank = 2 + o % 3
            def f1mm(t, o=o, bank=bank):
                last = None
                for c in range(NCH):
                    last = mm(t, bank, pb[bank][:, 0:256],
                              _t3(fc1_b.ap(), HID)[:, c, o * 128:(o + 1) * 128],
                              _t3(h2D.ap(), 256)[:, c, :], c == 0, c == NCH - 1)
                return last
            tf1 = p.emit("tensor", f1mm,
                         waits=[st["tok_h2"], tok_f1] + ps.wr_waits(bank), inc="mm")
            gl_last = p.emit("scalar", (lambda o, bank: lambda s: s.activation(
                _t3(geluD.ap(), 256)[:, o, :], pb[bank][:, 0:256], AF.Gelu,
                bias=f1c.ap()[:, o:o + 1]))(o, bank),
                waits=[tf1, gelu_free if o == 0 else None], inc="act")
            ps.note_read(bank, gl_last)
        h2_free = gl_last
        st["gl_last"] = gl_last

    def D_S3(it):
        nonlocal gelu_free
        bp = it % 2
        st = stD[it]
        ch_toks = []
        for o in range(NCH):
            bank = 5 + o % 2
            opar = o % 2
            def f2mm(t, o=o, bank=bank):
                last = None
                for c in range(32):
                    last = mm(t, bank, pb[bank][:, 0:256],
                              _t3(fc2_b.ap(), 1024)[:, c, o * 128:(o + 1) * 128],
                              _t3(geluD.ap(), 256)[:, c, :], c == 0, c == 31)
                return last
            tf2_ = p.emit("tensor", f2mm,
                          waits=[st["gl_last"], tok_f2] + ps.wr_waits(bank), inc="mm")
            t1 = p.emit("scalar", (lambda o, bank, opar: lambda s: s.activation(
                sfD[opar].ap(), pb[bank][:, 0:256], AF.Identity,
                bias=f2c.ap()[:, o:o + 1]))(o, bank, opar),
                waits=[tf2_, sfD_free[opar]], inc="act")
            ps.note_read(bank, t1)
            p.emit("vector", (lambda o, opar, bp: lambda v: v.tensor_tensor(
                sfD[opar].ap(), sfD[opar].ap(), _t3(y1t[bp].ap(), 256)[:, o, :],
                op=ALU.add))(o, opar, bp),
                waits=[t1], inc="dve")
            p.emit("vector", (lambda o, opar, bp: lambda v: v.tensor_tensor(
                sfD[opar].ap(), sfD[opar].ap(), _t3(gt[bp].ap(), 256)[:, o, :],
                op=ALU.mult))(o, opar, bp),
                waits=[st["tg_"]], inc="dve")
            t4 = p.emit("vector", (lambda o, opar, bp: lambda v: v.tensor_tensor(
                _t3(xf2[bp].ap(), 256)[:, o, :], sfD[opar].ap(),
                _t3(xf2[bp].ap(), 256)[:, o, :], op=ALU.add))(o, opar, bp),
                waits=[st["tx_"]], inc="dve")
            sfD_free[opar] = t4
            ch_toks.append(t4)
        gelu_free = p.last("mm")
        y1t_free[bp] = ch_toks[-1]
        gt_free[bp] = ch_toks[-1]
        tout = p.dma("sync", (lambda it, bp: lambda e: e.dma_start(
            out=out_d.rearrange("(a p) t -> p a t", p=128)[:, :, it * 256:(it + 1) * 256],
            in_=_t3(xf2[bp].ap(), 256)))(it, bp),
            waits=ch_toks, inc="sto")
        xf2_free[bp] = tout
        out_toks.append(tout)

    sq_slD = lambda c: _t3(sqD.ap(), 256)[:, c, :]
    if b2_tiles is not None:
        NT_B2 = b2_tiles
    D_S1(0)
    for it in range(NT_B2):
        D_S2(it)
        if it + 1 < NT_B2:
            D_S1(it + 1)
        D_S3(it)

    p.emit("sync", lambda e: None,
           waits=p.stream_last("sto") + p.stream_last("tap"))

    with nc.Block() as block:
        sems = p.make_sems(nc, stack)
        p.build(block, sems)
    stack.close()
    return nc


def _host_layout(inputs, seq=8192):
    x = np.asarray(inputs["L0_attn_x"], np.float32)
    qkv_w = np.asarray(inputs["L0_attn_qkv_w"], np.float32)
    proj_w = np.asarray(inputs["L0_attn_proj_w"], np.float32)
    fc1_w = np.asarray(inputs["L0_attn_fc1_w"], np.float32)
    fc2_w = np.asarray(inputs["L0_attn_fc2_w"], np.float32)
    rwv = np.asarray(inputs["L0_ffn_router_w"], np.float32)
    wg_r = np.asarray(inputs["L0_ffn_wg_r"], np.float32)
    wu_r = np.asarray(inputs["L0_ffn_wu_r"], np.float32)
    wd_r = np.asarray(inputs["L0_ffn_wd_r"], np.float32)
    wg_s = np.asarray(inputs["L0_ffn_wg_s"], np.float32)
    wu_s = np.asarray(inputs["L0_ffn_wu_s"], np.float32)
    wd_s = np.asarray(inputs["L0_ffn_wd_s"], np.float32)

    def col8(b):
        return np.ascontiguousarray(np.asarray(b, np.float32).reshape(8, 128).T)

    def col32(b):
        return np.ascontiguousarray(np.asarray(b, np.float32).reshape(32, 128).T)

    wg20 = np.concatenate([wg_r, np.stack([wg_s[0, :, 0:256], wg_s[0, :, 256:512],
                                           wg_s[1, :, 0:256], wg_s[1, :, 256:512]])])
    wu20 = np.concatenate([wu_r, np.stack([wu_s[0, :, 0:256], wu_s[0, :, 256:512],
                                           wu_s[1, :, 0:256], wu_s[1, :, 256:512]])])
    wd20 = np.concatenate([wd_r, np.stack([wd_s[0, 0:256], wd_s[0, 256:512],
                                           wd_s[1, 0:256], wd_s[1, 256:512]])])

    c_f32 = np.zeros((128, 8), np.float32)
    for i, v in enumerate([EPS_LN, EPS_ATTN, EPS_RTR, -1.0, -0.5, 1.0 / D, 0.0, 0.5]):
        c_f32[:, i] = v
    c_ident = np.eye(128, dtype=np.float32)
    c_ones1 = np.ones((128, 1), dtype=ml_dtypes.bfloat16)
    c_ones33 = np.zeros((33, 128), dtype=ml_dtypes.bfloat16)
    c_ones33[0, :] = 1
    c_ones33[32, :] = 1
    c_indz = np.zeros((48, NCH * 128), dtype=ml_dtypes.bfloat16)
    for c in range(NCH):
        for hh, base in ((2 * c, 0), (2 * c + 1, 64)):
            c_indz[hh, c * 128 + base: c * 128 + base + 64] = 1
            c_indz[32 + hh, c * 128 + base: c * 128 + base + 64] = 1
    c_inde = np.zeros((16, 16 * 128), dtype=ml_dtypes.bfloat16)
    for e in range(16):
        c_inde[e, e * 128:(e + 1) * 128] = 1
    c_pmv = np.zeros((33, 2), dtype=ml_dtypes.bfloat16)
    c_pmv[0, 0] = 1; c_pmv[32, 0] = 1
    c_pmv[0, 1] = -1; c_pmv[32, 1] = -1
    c_zrv = np.zeros((1, 128), dtype=ml_dtypes.bfloat16)

    shared = dict(
        wqT=np.ascontiguousarray(qkv_w[0:1024].T),
        wkvT=np.ascontiguousarray(qkv_w[1024:3072].T),
        projT=np.ascontiguousarray(proj_w.T),
        fc1T=np.ascontiguousarray(fc1_w.T),
        fc2T=np.ascontiguousarray(fc2_w.T),
        rw=np.ascontiguousarray(rwv),
        wg=np.ascontiguousarray(wg20), wu=np.ascontiguousarray(wu20),
        wd=np.ascontiguousarray(wd20),
        b_n1g=col8(inputs["L0_attn_norm1_g"]), b_n1b=col8(inputs["L0_attn_norm1_b"]),
        b_n2g=col8(inputs["L0_attn_norm2_g"]), b_n2b=col8(inputs["L0_attn_norm2_b"]),
        b_pj=col8(inputs["L0_attn_proj_b"]), b_fc2=col8(inputs["L0_attn_fc2_b"]),
        b_fc1=col32(inputs["L0_attn_fc1_b"]),
        c_f32=c_f32, c_ident=c_ident, c_ones1=c_ones1, c_ones33=c_ones33,
        c_indz=c_indz, c_inde=c_inde, c_pm=c_pmv, c_zr=c_zrv,
        c_i1=np.ones((1, 1), np.int32))

    in_maps = []
    T = seq // 2
    for core in range(8):
        bidx, half = core // 2, core % 2
        xt = x[bidx, :seq].T
        own = xt[:, half * T:(half + 1) * T]
        oth = xt[:, (1 - half) * T:(2 - half) * T]
        m = dict(shared)
        m["xT"] = np.ascontiguousarray(np.concatenate([own, oth], axis=1))
        in_maps.append(m)
    return in_maps


def kernel(**inputs):
    seq = 8192
    in_maps = _host_layout(inputs, seq)
    nc = build_core(seq)
    res = run_bass_kernel_spmd(nc, in_maps, core_ids=list(range(8)))
    T = seq // 2
    out = np.zeros((4, seq, D), np.float32)
    for core in range(8):
        bidx, half = core // 2, core % 2
        oc = np.asarray(res.results[core]["out"])
        out[bidx, half * T:(half + 1) * T, :] = oc.T
    return out
